# version 01
# baseline (speedup 1.0000x reference)
import numpy as np
import ml_dtypes
import concourse.bass as bass
import concourse.mybir as mybir
from concourse.bass import IndirectOffsetOnAxis
from concourse.bass_utils import run_bass_kernel_spmd
from contextlib import ExitStack

F32 = mybir.dt.float32
BF16 = mybir.dt.bfloat16
U32 = mybir.dt.uint32
AF = mybir.ActivationFunctionType
ALU = mybir.AluOpType
AX = mybir.AxisListType

NCORES = 8
B_PER = 4
S = 2048
C = 1024
H = 16
NB = S // 128
NTOK = NCORES * B_PER * S
E = 8
K_SEL = 2048
DFF = 4096
EXT = 48
SEC = 384
NOPS = SEC // 128
BIS_IT = 45
ATT_SCALE = C ** -0.5
LN_EPS = 1e-5


class Plan:
    def __init__(self):
        self.ops = {k: [] for k in ("gpsimd", "sync", "tensor", "vector", "scalar")}
        self.ctr = {}

    def cnt(self, sem):
        return self.ctr.get(id(sem), 0)

    def emit(self, eng, fn, waits=(), inc=None):
        if inc is not None:
            sem, amt = inc
            self.ctr[id(sem)] = self.ctr.get(id(sem), 0) + amt
        self.ops[eng].append((fn, tuple(waits), inc))

    def run(self, eng_name, eng):
        for fn, waits, inc in self.ops[eng_name]:
            for sem, val in waits:
                if val > 0:
                    eng.wait_ge(sem, val)
            inst = fn(eng)
            if inc is not None and inst is not None:
                inst.then_inc(inc[0], inc[1])


def attach(plan, block):
    @block.gpsimd
    def _(e):
        plan.run("gpsimd", e)

    @block.vector
    def _(e):
        plan.run("vector", e)

    @block.scalar
    def _(e):
        plan.run("scalar", e)

    @block.tensor
    def _(e):
        plan.run("tensor", e)

    @block.sync
    def _(e):
        plan.run("sync", e)


def mkhelpers(P, sT, sV, sA):
    def g(fn, sem, waits=()):
        P.emit("gpsimd", fn, waits, (sem, 16))

    def so(fn, sem, waits=()):
        P.emit("sync", fn, waits, (sem, 16))

    def t(fn, waits=(), inc=True):
        P.emit("tensor", fn, waits, (sT, 1) if inc else None)

    def v(fn, waits=()):
        P.emit("vector", fn, waits, (sV, 1))

    def a(fn, waits=()):
        P.emit("scalar", fn, waits, (sA, 1))

    return g, so, t, v, a


def emit_ln(P, g, so, t, v, a, sems, bufs, src_dma_fn, dst_dma_fn, extra_w_v=0,
            pre_add=None):
    dL, dS, sT, sV, sA = sems
    z, sqj, mean_t, ss_t, lv_t, rstd_t, x1t = (
        bufs["z"], bufs["sqj"], bufs["mean"], bufs["ss"], bufs["lv"],
        bufs["rstd"], bufs["out"])
    g(src_dma_fn, waits=[(sV, P.cnt(sV))])
    ld_at = P.cnt(dL)
    w0 = [(dL, ld_at), (sV, max(P.cnt(sV), extra_w_v))]
    if pre_add is not None:
        v(pre_add, waits=w0)
        w0 = [(sV, P.cnt(sV))]
    v(lambda e: e.reduce_sum(mean_t[:, :], z[:], axis=AX.X), waits=w0)
    v(lambda e: e.tensor_scalar(out=mean_t[:, :], in0=mean_t[:, :],
                                scalar1=1.0 / C, scalar2=None, op0=ALU.mult),
      waits=[(sV, P.cnt(sV))])
    v(lambda e: e.tensor_scalar(out=z[:], in0=z[:], scalar1=mean_t[:, 0:1],
                                scalar2=None, op0=ALU.subtract),
      waits=[(sV, P.cnt(sV))])
    zc_v = P.cnt(sV)
    a(lambda e: e.activation(out=sqj[:, :], in_=z[:], func=AF.Square,
                             accum_out=ss_t[:, :]),
      waits=[(sV, zc_v), (sA, P.cnt(sA))])
    a(lambda e: e.activation(out=lv_t[:, :], in_=ss_t[:, :], func=AF.Ln,
                             scale=1.0 / C, bias=LN_EPS),
      waits=[(sA, P.cnt(sA))])
    a(lambda e: e.activation(out=rstd_t[:, :], in_=lv_t[:, :], func=AF.Exp,
                             scale=-0.5),
      waits=[(sA, P.cnt(sA))])
    rstd_a = P.cnt(sA)
    v(lambda e: e.tensor_scalar(out=x1t[:], in0=z[:], scalar1=rstd_t[:, 0:1],
                                scalar2=None, op0=ALU.mult),
      waits=[(sA, rstd_a), (sV, P.cnt(sV)), (dS, bufs.get("out_store", 0))])
    x1_v = P.cnt(sV)
    so(dst_dma_fn, waits=[(sV, x1_v)])
    return x1_v


def build_l1(nbat=B_PER, upto=6):
    nc = bass.Bass("TRN2", target_bir_lowering=False, debug=False)
    xin = nc.dram_tensor("xin", [B_PER, S, C], F32, kind="ExternalInput").ap()
    wq = nc.dram_tensor("wq", [C, C], F32, kind="ExternalInput").ap()
    wk = nc.dram_tensor("wk", [C, C], F32, kind="ExternalInput").ap()
    bq_r = nc.dram_tensor("bq_r", [128, 8], F32, kind="ExternalInput").ap()
    bk_r = nc.dram_tensor("bk_r", [128, 8], F32, kind="ExternalInput").ap()
    rw = nc.dram_tensor("rw", [C, E], F32, kind="ExternalInput").ap()
    rb_r = nc.dram_tensor("rb_r", [E, 1], F32, kind="ExternalInput").ap()
    idn_in = nc.dram_tensor("idn_in", [128, 128], F32, kind="ExternalInput").ap()
    ones_in = nc.dram_tensor("ones_in", [128, 8], F32, kind="ExternalInput").ap()

    x1f_out = nc.dram_tensor("x1f", [B_PER * S, C], F32, kind="ExternalOutput").ap()
    x1b_out = nc.dram_tensor("x1b", [B_PER * S, C], BF16, kind="ExternalOutput").ap()
    sc_out = nc.dram_tensor("sc", [E, B_PER * S], F32, kind="ExternalOutput").ap()

    P = Plan()
    st = ExitStack()
    En = st.enter_context
    xT = En(nc.sbuf_tensor("xT", [128, 8 * 2048], BF16))
    kT = En(nc.sbuf_tensor("kT", [128, 8 * 2048], BF16))
    v_sb = En(nc.sbuf_tensor("v_sb", [128, NB * H * 65], BF16))
    qTs = En(nc.sbuf_tensor("qTs", [128, 512], BF16))
    P_sb = En(nc.sbuf_tensor("P_sb", [128, NB * 512], BF16))
    att = En(nc.sbuf_tensor("att", [128, 4 * C], F32))
    x1T = En(nc.sbuf_tensor("x1T", [128, 8 * 512], BF16))
    wqs = [En(nc.sbuf_tensor(f"wqs{i}", [128, 1024], BF16)) for i in range(2)]
    wks = [En(nc.sbuf_tensor(f"wks{i}", [128, 1024], BF16)) for i in range(2)]
    rw_sb = En(nc.sbuf_tensor("rw_sb", [128, 8 * E], BF16))
    bq_sb = En(nc.sbuf_tensor("bq_sb", [128, 8], F32))
    bk_sb = En(nc.sbuf_tensor("bk_sb", [128, 8], F32))
    rb_sb = En(nc.sbuf_tensor("rb_sb", [E, 1], F32))
    idn = En(nc.sbuf_tensor("idn", [128, 128], BF16))
    idn32 = En(nc.sbuf_tensor("idn32", [128, 128], F32))
    ones8 = En(nc.sbuf_tensor("ones8", [8, 1], F32))
    ones1_8 = En(nc.sbuf_tensor("ones1_8", [1, 8], F32))
    xf32 = [En(nc.sbuf_tensor(f"xf32_{i}", [128, C], F32)) for i in range(2)]
    xbf = [En(nc.sbuf_tensor(f"xbf_{i}", [128, C], BF16)) for i in range(2)]
    xres = [En(nc.sbuf_tensor(f"xres_{i}", [128, C], F32)) for i in range(2)]
    sqj = En(nc.sbuf_tensor("sqj", [128, C], F32))
    x1t = [En(nc.sbuf_tensor(f"x1t{i}", [128, C], F32)) for i in range(2)]
    x1bt = [En(nc.sbuf_tensor(f"x1bt{i}", [128, C], BF16)) for i in range(2)]
    mean_t = En(nc.sbuf_tensor("mean_t", [128, 1], F32))
    ss_t = En(nc.sbuf_tensor("ss_t", [128, 1], F32))
    lv_t = En(nc.sbuf_tensor("lv_t", [128, 1], F32))
    rstd_t = En(nc.sbuf_tensor("rstd_t", [128, 1], F32))
    eps_t = En(nc.sbuf_tensor("eps_t", [128, 1], F32))
    u_sb = En(nc.sbuf_tensor("u_sb", [65, 512], F32))
    rs_sb = En(nc.sbuf_tensor("rs_sb", [128, 4], F32))
    p8 = En(nc.sbuf_tensor("p8", [E, 512], F32))
    rS8 = En(nc.sbuf_tensor("rS8", [1, 512], F32))
    scn = En(nc.sbuf_tensor("scn", [E, 512], F32))

    ps_tr = En(nc.psum_tensor("ps_tr", [128, 1024], BF16))
    ps_pj = En(nc.psum_tensor("ps_pj", [128, 512], F32))
    ps_L = En(nc.psum_tensor("ps_L", [128, 2048], F32))
    ps_AVU = En(nc.psum_tensor("ps_AVU", [128, 512], F32))
    ps_r = En(nc.psum_tensor("ps_r", [8, 512], F32))

    dC = En(nc.semaphore("dC"))
    dX = [En(nc.semaphore(f"dX{i}")) for i in range(2)]
    dWk = [En(nc.semaphore(f"dWk{i}")) for i in range(2)]
    dWq = [En(nc.semaphore(f"dWq{i}")) for i in range(2)]
    dXr = [En(nc.semaphore(f"dXr{i}")) for i in range(2)]
    oX1f = [En(nc.semaphore(f"oX1f{i}")) for i in range(2)]
    oX1b = [En(nc.semaphore(f"oX1b{i}")) for i in range(2)]
    oSc = En(nc.semaphore("oSc"))
    sT = En(nc.semaphore("sT"))
    sV = En(nc.semaphore("sV"))
    sA = En(nc.semaphore("sA"))
    block = En(nc.Block())

    g, so, t, v, a = mkhelpers(P, sT, sV, sA)

    g(lambda e: e.dma_start(out=rw_sb[:].rearrange("p (cc d) -> p cc d", cc=8),
                            in_=rw.rearrange("(cc p) d -> p cc d", p=128)), dC)
    g(lambda e: e.dma_start(out=bq_sb[:], in_=bq_r[:, :]), dC)
    g(lambda e: e.dma_start(out=bk_sb[:], in_=bk_r[:, :]), dC)
    g(lambda e: e.dma_start(out=rb_sb[:], in_=rb_r[:, :]), dC)
    g(lambda e: e.dma_start(out=idn[:], in_=idn_in[:, :]), dC)
    g(lambda e: e.dma_start(out=idn32[:], in_=idn_in[:, :]), dC)
    g(lambda e: e.dma_start(out=ones8[:], in_=ones_in[0:8, 0:1],
                            allow_slow_non_contiguous=True), dC)
    g(lambda e: e.dma_start(out=ones1_8[:], in_=ones_in[0:1, 0:8]), dC)
    CONST = P.cnt(dC)

    v(lambda e: e.memset(
        v_sb[:].rearrange("p (m h y) -> p m h y", m=NB, h=H)[:, :, :, 64:65], 1.0))
    ONES_V = P.cnt(sV)
    v(lambda e: e.memset(eps_t[:], LN_EPS))

    prev_scn_v = [0]

    for b in range(nbat):
        for tb in range(NB):
            r = tb % 2
            g(lambda e, b=b, tb=tb, r=r: e.dma_start(
                out=xf32[r][:], in_=xin[b, tb * 128:(tb + 1) * 128, :]),
              dX[r], waits=[(sV, P.cnt(sV))])
            ld_at = P.cnt(dX[r])
            v(lambda e, r=r: e.tensor_copy(xbf[r][:], xf32[r][:]),
              waits=[(dX[r], ld_at), (sT, P.cnt(sT))])
            cast_at = P.cnt(sV)
            for k in range(8):
                t(lambda e, r=r, k=k: e.transpose(
                    ps_tr[:, k * 128:(k + 1) * 128], xbf[r][:, k * 128:(k + 1) * 128],
                    idn[:, :]),
                  waits=([(sV, cast_at), (dC, CONST)] if k == 0 else []),
                  inc=(k == 7))
            tr_at = P.cnt(sT)
            v(lambda e, tb=tb: e.tensor_copy(
                xT[:].rearrange("p (cc n) -> p cc n", cc=8)[:, :, tb * 128:tb * 128 + 128],
                ps_tr[:].rearrange("p (cc k) -> p cc k", cc=8)),
              waits=[(sT, tr_at)])
        xT_v = P.cnt(sV)
        if upto < 3:
            continue

        for dblk in range(8):
            r = dblk % 2
            g(lambda e, dblk=dblk, r=r: e.dma_start(
                out=wks[r][:].rearrange("p (cc d) -> p cc d", cc=8),
                in_=wk[:, dblk * 128:(dblk + 1) * 128].rearrange(
                    "(cc p) d -> p cc d", p=128)),
              dWk[r], waits=[(sT, P.cnt(sT))] if dblk >= 2 else [])
            wk_at = P.cnt(dWk[r])
            for ncn in range(4):
                evac_prior = P.cnt(sV)
                for cc in range(8):
                    t(lambda e, r=r, ncn=ncn, cc=cc: e.matmul(
                        ps_pj[:, :],
                        wks[r][:, cc * 128:(cc + 1) * 128],
                        xT[:, cc * 2048 + ncn * 512: cc * 2048 + ncn * 512 + 512],
                        start=(cc == 0), stop=(cc == 7)),
                      waits=([(sV, max(xT_v, evac_prior)), (dWk[r], wk_at)]
                             if cc == 0 else []),
                      inc=(cc == 7))
                mm_at = P.cnt(sT)
                v(lambda e, dblk=dblk, ncn=ncn: e.tensor_scalar(
                    out=kT[:, dblk * 2048 + ncn * 512: dblk * 2048 + ncn * 512 + 512],
                    in0=ps_pj[:, :], scalar1=bk_sb[:, dblk:dblk + 1], scalar2=None,
                    op0=ALU.add),
                  waits=[(sT, mm_at)])
        kT_v = P.cnt(sV)
        if upto < 4:
            continue

        for mb in range(NB):
            evac_prior = P.cnt(sV)
            for j in range(8):
                t(lambda e, j=j, mb=mb: e.transpose(
                    ps_tr[:, j * 128:(j + 1) * 128],
                    kT[:, j * 2048 + mb * 128: j * 2048 + mb * 128 + 128],
                    idn[:, :]),
                  waits=([(sV, max(kT_v, evac_prior))] if j == 0 else []),
                  inc=(j == 7))
            tr_at = P.cnt(sT)
            v(lambda e, mb=mb: e.tensor_copy(
                v_sb[:, mb * H * 65:(mb + 1) * H * 65].rearrange(
                    "p (h y) -> p h y", h=H)[:, :, 0:64],
                ps_tr[:].rearrange("p (h y) -> p h y", h=H)),
              waits=[(sT, tr_at)])
        v_v = P.cnt(sV)
        if upto < 5:
            continue

        for ncn in range(4):
            for hp in range(8):
                r = hp % 2
                g(lambda e, hp=hp, r=r: e.dma_start(
                    out=wqs[r][:].rearrange("p (cc d) -> p cc d", cc=8),
                    in_=wq[:, hp * 128:(hp + 1) * 128].rearrange(
                        "(cc p) d -> p cc d", p=128)),
                  dWq[r], waits=[(sT, P.cnt(sT))] if (b + ncn + hp) > 0 else [])
                wq_at = P.cnt(dWq[r])
                for cc in range(8):
                    t(lambda e, r=r, cc=cc, ncn=ncn: e.matmul(
                        ps_pj[:, :],
                        wqs[r][:, cc * 128:(cc + 1) * 128],
                        xT[:, cc * 2048 + ncn * 512: cc * 2048 + ncn * 512 + 512],
                        start=(cc == 0), stop=(cc == 7)),
                      waits=([(dWq[r], wq_at), (sV, P.cnt(sV))] if cc == 0 else []),
                      inc=(cc == 7))
                mm_at = P.cnt(sT)
                v(lambda e, hp=hp: e.tensor_scalar(
                    out=qTs[:, :], in0=ps_pj[:, :],
                    scalar1=bq_sb[:, hp:hp + 1], scalar2=None, op0=ALU.add),
                  waits=[(sT, mm_at)])
                qts_v = P.cnt(sV)

                for hh in range(2):
                    h = 2 * hp + hh
                    rb_ = hh * 64
                    for mg in range(4):
                        exp_prior = P.cnt(sA)
                        for mi in range(4):
                            mb = mg * 4 + mi
                            t(lambda e, rb_=rb_, hp=hp, mb=mb, mi=mi: e.matmul(
                                ps_L[:, mi * 512:(mi + 1) * 512],
                                kT[rb_:rb_ + 64,
                                   hp * 2048 + mb * 128: hp * 2048 + mb * 128 + 128],
                                qTs[rb_:rb_ + 64, :],
                                start=True, stop=True),
                              waits=([(sA, exp_prior), (sV, qts_v)] if mi == 0 else []),
                              inc=(mi == 3))
                        lg_at = P.cnt(sT)
                        a(lambda e, mg=mg: e.activation(
                            out=P_sb[:, mg * 2048:(mg + 1) * 2048],
                            in_=ps_L[:, :], func=AF.Exp, scale=ATT_SCALE),
                          waits=[(sT, lg_at)])
                    exp_at = P.cnt(sA)
                    dve_prior = P.cnt(sV)
                    for mb in range(NB):
                        t(lambda e, mb=mb, h=h: e.matmul(
                            ps_AVU[0:65, :],
                            v_sb[:, mb * H * 65 + h * 65: mb * H * 65 + h * 65 + 65],
                            P_sb[:, mb * 512:(mb + 1) * 512],
                            start=(mb == 0), stop=(mb == NB - 1)),
                          waits=([(sA, exp_at),
                                  (sV, max(v_v, dve_prior, ONES_V))] if mb == 0 else []),
                          inc=(mb == NB - 1))
                    av_at = P.cnt(sT)
                    v(lambda e: e.tensor_copy(u_sb[:, :], ps_AVU[0:65, :]),
                      waits=[(sT, av_at)])
                    u_v = P.cnt(sV)
                    for q in range(4):
                        t(lambda e, q=q: e.transpose(
                            ps_AVU[:, q * 65:(q + 1) * 65],
                            u_sb[0:65, q * 128:(q + 1) * 128],
                            idn32[0:65, 0:65]),
                          waits=([(sV, u_v)] if q == 0 else []),
                          inc=(q == 3))
                    ut_at = P.cnt(sT)
                    v(lambda e: e.reciprocal(
                        rs_sb[:, :],
                        ps_AVU[:, 0:260].rearrange("p (q y) -> p q y", q=4)[:, :, 64:65]),
                      waits=[(sT, ut_at)])
                    rs_v = P.cnt(sV)
                    v(lambda e, h=h: e.tensor_tensor(
                        out=att[:].rearrange("p (q c) -> p q c", q=4)[:, :, h * 64:h * 64 + 64],
                        in0=ps_AVU[:, 0:260].rearrange("p (q y) -> p q y", q=4)[:, :, 0:64],
                        in1=rs_sb[:, :].rearrange("p (q o) -> p q o", o=1).to_broadcast(
                            [128, 4, 64]),
                        op=ALU.mult),
                      waits=[(sV, rs_v)])
            att_v = P.cnt(sV)
            if upto < 6:
                continue

            for t4 in range(4):
                tb = ncn * 4 + t4
                r = t4 % 2
                g(lambda e, b=b, tb=tb, r=r: e.dma_start(
                    out=xres[r][:], in_=xin[b, tb * 128:(tb + 1) * 128, :]),
                  dXr[r], waits=[(sV, P.cnt(sV))])
                res_at = P.cnt(dXr[r])
                v(lambda e, r=r, t4=t4: e.tensor_tensor(
                    out=xres[r][:], in0=att[:, t4 * C:(t4 + 1) * C], in1=xres[r][:],
                    op=ALU.add),
                  waits=[(dXr[r], res_at), (sV, att_v)])
                v(lambda e, r=r: e.reduce_sum(mean_t[:, :], xres[r][:], axis=AX.X),
                  waits=[(sV, P.cnt(sV))])
                v(lambda e: e.tensor_scalar(out=mean_t[:, :], in0=mean_t[:, :],
                                            scalar1=1.0 / C, scalar2=None, op0=ALU.mult),
                  waits=[(sV, P.cnt(sV))])
                v(lambda e, r=r: e.tensor_scalar(out=xres[r][:], in0=xres[r][:],
                                                 scalar1=mean_t[:, 0:1], scalar2=None,
                                                 op0=ALU.subtract),
                  waits=[(sV, P.cnt(sV))])
                zc_v = P.cnt(sV)
                a(lambda e, r=r: e.activation(out=sqj[:, :], in_=xres[r][:],
                                              func=AF.Square, accum_out=ss_t[:, :]),
                  waits=[(sV, zc_v), (sA, P.cnt(sA))])
                a(lambda e: e.activation(out=lv_t[:, :], in_=ss_t[:, :],
                                         func=AF.Ln, scale=1.0 / C, bias=eps_t[:, 0:1]),
                  waits=[(sA, P.cnt(sA))])
                a(lambda e: e.activation(out=rstd_t[:, :], in_=lv_t[:, :],
                                         func=AF.Exp, scale=-0.5),
                  waits=[(sA, P.cnt(sA))])
                rstd_a = P.cnt(sA)
                v(lambda e, r=r: e.tensor_scalar(out=x1t[r][:], in0=xres[r][:],
                                                 scalar1=rstd_t[:, 0:1], scalar2=None,
                                                 op0=ALU.mult),
                  waits=[(sA, rstd_a), (sV, P.cnt(sV)), (oX1f[r], P.cnt(oX1f[r]))])
                x1_v = P.cnt(sV)
                v(lambda e, r=r: e.tensor_copy(x1bt[r][:], x1t[r][:]),
                  waits=[(sV, x1_v), (oX1b[r], P.cnt(oX1b[r]))])
                x1b_v = P.cnt(sV)
                so(lambda e, b=b, tb=tb, r=r: e.dma_start(
                    out=x1f_out[b * S + tb * 128: b * S + (tb + 1) * 128, :],
                    in_=x1t[r][:]),
                   oX1f[r], waits=[(sV, x1_v)])
                so(lambda e, b=b, tb=tb, r=r: e.dma_start(
                    out=x1b_out[b * S + tb * 128: b * S + (tb + 1) * 128, :],
                    in_=x1bt[r][:]),
                   oX1b[r], waits=[(sV, x1b_v)])
                for k in range(8):
                    t(lambda e, r=r, k=k: e.transpose(
                        ps_tr[:, k * 128:(k + 1) * 128],
                        x1bt[r][:, k * 128:(k + 1) * 128], idn[:, :]),
                      waits=([(sV, x1b_v)] if k == 0 else []),
                      inc=(k == 7))
                xtr_at = P.cnt(sT)
                v(lambda e, t4=t4: e.tensor_copy(
                    x1T[:].rearrange("p (cc n) -> p cc n", cc=8)[:, :, t4 * 128:(t4 + 1) * 128],
                    ps_tr[:].rearrange("p (cc k) -> p cc k", cc=8)),
                  waits=[(sT, xtr_at)])
            x1T_v = P.cnt(sV)

            for cc in range(8):
                t(lambda e, cc=cc: e.matmul(
                    ps_r[:, :],
                    rw_sb[:, cc * E:(cc + 1) * E],
                    x1T[:, cc * 512:(cc + 1) * 512],
                    start=(cc == 0), stop=(cc == 7)),
                  waits=([(sV, x1T_v), (sA, P.cnt(sA))] if cc == 0 else []),
                  inc=(cc == 7))
            rmm_at = P.cnt(sT)
            a(lambda e: e.activation(out=p8[:, :], in_=ps_r[:, :],
                                     func=AF.Exp, bias=rb_sb[:, 0:1]),
              waits=[(sT, rmm_at), (sA, P.cnt(sA)), (sV, prev_scn_v[0])])
            p8_a = P.cnt(sA)
            t(lambda e: e.matmul(ps_r[0:1, :], ones8[:, :], p8[:, :],
                                 start=True, stop=True),
              waits=[(sA, p8_a)])
            smm_at = P.cnt(sT)
            v(lambda e: e.reciprocal(rS8[:, :], ps_r[0:1, :]),
              waits=[(sT, smm_at)])
            rs8_v = P.cnt(sV)
            t(lambda e: e.matmul(ps_r[:, :], ones1_8[:, :], rS8[:, :],
                                 start=True, stop=True),
              waits=[(sV, rs8_v)])
            bc_at = P.cnt(sT)
            v(lambda e: e.tensor_tensor(out=scn[:, :], in0=p8[:, :], in1=ps_r[:, :],
                                        op=ALU.mult),
              waits=[(sT, bc_at), (oSc, P.cnt(oSc))])
            scn_v = P.cnt(sV)
            prev_scn_v[0] = scn_v
            so(lambda e, b=b, ncn=ncn: e.dma_start(
                out=sc_out[:, b * S + ncn * 512: b * S + (ncn + 1) * 512],
                in_=scn[:, :]),
               oSc, waits=[(sV, scn_v)])

    P.emit("sync", lambda e: None,
           waits=[(oX1f[0], P.cnt(oX1f[0])), (oX1f[1], P.cnt(oX1f[1])),
                  (oX1b[0], P.cnt(oX1b[0])), (oX1b[1], P.cnt(oX1b[1])),
                  (oSc, P.cnt(oSc))], inc=None)
    attach(P, block)
    st.close()
    return nc
def build_l2():
    nc = bass.Bass("TRN2", target_bir_lowering=False, debug=False)
    sc_in = nc.dram_tensor("sc", [128, 512], F32, kind="ExternalInput").ap()
    x1b_in = nc.dram_tensor("x1b", [NTOK, C], BF16, kind="ExternalInput").ap()
    w1_in = nc.dram_tensor("w1", [C, DFF], F32, kind="ExternalInput").ap()
    b1_r = nc.dram_tensor("b1_r", [128, 32], F32, kind="ExternalInput").ap()
    w2_in = nc.dram_tensor("w2", [DFF, C], F32, kind="ExternalInput").ap()
    b2_r = nc.dram_tensor("b2_r", [1, C], F32, kind="ExternalInput").ap()
    idn_in = nc.dram_tensor("idn_in", [128, 128], F32, kind="ExternalInput").ap()
    cst_in = nc.dram_tensor("cst", [128, 24], F32, kind="ExternalInput").ap()
    ut_in = nc.dram_tensor("ut_in", [128, 128], F32, kind="ExternalInput").ap()
    gut_in = nc.dram_tensor("gut_in", [128, 128], F32, kind="ExternalInput").ap()
    gind_in = nc.dram_tensor("gind_in", [128, 8], F32, kind="ExternalInput").ap()
    onesr_in = nc.dram_tensor("onesr", [1, 128], F32, kind="ExternalInput").ap()

    secs_out = nc.dram_tensor("secs", [E * SEC, 1028], F32, kind="ExternalOutput").ap()
    cnts_out = nc.dram_tensor("cnts", [E, 1], F32, kind="ExternalOutput").ap()

    extbuf = nc.dram_tensor("extbuf", [128 * EXT, 4], F32)

    P = Plan()
    st = ExitStack()
    En = st.enter_context
    sc = En(nc.sbuf_tensor("sc_sb", [128, 512], F32))
    work = En(nc.sbuf_tensor("work", [128, 512], F32))
    vals = En(nc.sbuf_tensor("vals", [128, EXT], F32))
    colu = En(nc.sbuf_tensor("colu", [128, EXT], U32))
    colf = En(nc.sbuf_tensor("colf", [128, EXT], F32))
    tok = En(nc.sbuf_tensor("tok", [128, EXT], F32))
    lo_t = En(nc.sbuf_tensor("lo_t", [128, 1], F32))
    hi_t = En(nc.sbuf_tensor("hi_t", [128, 1], F32))
    mid_t = En(nc.sbuf_tensor("mid_t", [128, 1], F32))
    cntp = En(nc.sbuf_tensor("cntp", [128, 1], F32))
    cbc = En(nc.sbuf_tensor("cbc", [128, 1], F32))
    selm = En(nc.sbuf_tensor("selm", [128, 1], U32))
    mgt = En(nc.sbuf_tensor("mgt", [128, EXT], F32))
    meq = En(nc.sbuf_tensor("meq", [128, EXT], F32))
    scr = En(nc.sbuf_tensor("scr", [128, EXT], F32))
    sel = En(nc.sbuf_tensor("sel", [128, EXT], F32))
    rkeq = En(nc.sbuf_tensor("rkeq", [128, EXT], F32))
    zer48 = En(nc.sbuf_tensor("zer48", [128, EXT], F32))
    cgt = En(nc.sbuf_tensor("cgt", [128, 1], F32))
    ceq = En(nc.sbuf_tensor("ceq", [128, 1], F32))
    csel = En(nc.sbuf_tensor("csel", [128, 1], F32))
    beq = En(nc.sbuf_tensor("beq", [128, 1], F32))
    selb = En(nc.sbuf_tensor("selb", [128, 1], F32))
    grpr = En(nc.sbuf_tensor("grpr", [128, 1], F32))
    secp = En(nc.sbuf_tensor("secp", [128, EXT], F32))
    c1b = En(nc.sbuf_tensor("c1b", [128, 1], F32))
    qb = En(nc.sbuf_tensor("qb", [128, 1], F32))
    cnt8 = En(nc.sbuf_tensor("cnt8", [8, 1], F32))
    ext4 = En(nc.sbuf_tensor("ext4", [128, EXT * 4], F32))
    sbb = En(nc.sbuf_tensor("sbb", [128, 128], F32))
    sbrow = En(nc.sbuf_tensor("sbrow", [1, 128], F32))
    cmpm = En(nc.sbuf_tensor("cmpm", [128, 128], F32))
    pcnt = En(nc.sbuf_tensor("pcnt", [128, 1], F32))
    bof = En(nc.sbuf_tensor("bof", [128, 1], F32))
    srcf = En(nc.sbuf_tensor("srcf", [128, 16], F32))
    srcu = En(nc.sbuf_tensor("srcu", [128, 16], U32))
    cmp4 = En(nc.sbuf_tensor("cmp4", [128, 64], F32))
    toku = En(nc.sbuf_tensor("toku", [128, 16], U32))
    secu = En(nc.sbuf_tensor("secu", [128, 16], U32))
    toks = [En(nc.sbuf_tensor(f"toks{i}", [128, C], BF16)) for i in range(2)]
    toksT = En(nc.sbuf_tensor("toksT", [128, 8 * 2048], BF16))
    h1T = En(nc.sbuf_tensor("h1T", [128, 32 * 1024], BF16))
    w2_sb = En(nc.sbuf_tensor("w2_sb", [128, 32 * 1024], BF16))
    w1s = [En(nc.sbuf_tensor(f"w1s{i}", [128, 1024], BF16)) for i in range(2)]
    b1_sb = En(nc.sbuf_tensor("b1_sb", [128, 32], F32))
    b2_sb = En(nc.sbuf_tensor("b2_sb", [1, C], BF16))
    onesc = En(nc.sbuf_tensor("onesc", [1, 128], BF16))
    idn = En(nc.sbuf_tensor("idn", [128, 128], BF16))
    idn32 = En(nc.sbuf_tensor("idn32", [128, 128], F32))
    ut_sb = En(nc.sbuf_tensor("ut_sb", [128, 128], F32))
    gut_sb = En(nc.sbuf_tensor("gut_sb", [128, 128], F32))
    gind_sb = En(nc.sbuf_tensor("gind_sb", [128, 8], F32))
    cst = En(nc.sbuf_tensor("cst_sb", [128, 24], F32))
    ones128 = En(nc.sbuf_tensor("ones128", [128, 1], F32))
    ones1_128 = En(nc.sbuf_tensor("ones1_128", [1, 128], F32))
    contrib = [En(nc.sbuf_tensor(f"contrib{i}", [128, 1028], F32)) for i in range(2)]

    ps_m = En(nc.psum_tensor("ps_m", [128, 512], F32))
    ps_t = En(nc.psum_tensor("ps_t", [128, 1024], BF16))
    ps_h1 = En(nc.psum_tensor("ps_h1", [128, 1024], F32))
    ps_eo = En(nc.psum_tensor("ps_eo", [128, 1024], F32))

    dC = En(nc.semaphore("dC"))
    dW1 = [En(nc.semaphore(f"dW1_{i}")) for i in range(2)]
    dGc = En(nc.semaphore("dGc"))
    dGx = [En(nc.semaphore(f"dGx{i}")) for i in range(2)]
    dSc = [En(nc.semaphore(f"dSc{i}")) for i in range(2)]
    oExt = En(nc.semaphore("oExt"))
    oCnt = En(nc.semaphore("oCnt"))
    sT = En(nc.semaphore("sT"))
    sV = En(nc.semaphore("sV"))
    sA = En(nc.semaphore("sA"))
    block = En(nc.Block())

    g, so, t, v, a = mkhelpers(P, sT, sV, sA)

    g(lambda e: e.dma_start(out=sc[:], in_=sc_in[:, :]), dC)
    g(lambda e: e.dma_start(out=idn[:], in_=idn_in[:, :]), dC)
    g(lambda e: e.dma_start(out=idn32[:], in_=idn_in[:, :]), dC)
    g(lambda e: e.dma_start(out=ut_sb[:], in_=ut_in[:, :]), dC)
    g(lambda e: e.dma_start(out=gut_sb[:], in_=gut_in[:, :]), dC)
    g(lambda e: e.dma_start(out=gind_sb[:], in_=gind_in[:, :]), dC)
    g(lambda e: e.dma_start(out=cst[:], in_=cst_in[:, :]), dC)
    g(lambda e: e.dma_start(out=ones128[:], in_=cst_in[:, 18:19],
                            allow_slow_non_contiguous=True), dC)
    g(lambda e: e.dma_start(out=ones1_128[:], in_=onesr_in[:, :]), dC)
    g(lambda e: e.dma_start(out=b1_sb[:], in_=b1_r[:, :]), dC)
    g(lambda e: e.dma_start(out=b2_sb[:], in_=b2_r[:, :]), dC)
    g(lambda e: e.dma_start(out=onesc[:], in_=onesr_in[:, :]), dC)
    g(lambda e: e.dma_start(out=w2_sb[:].rearrange("p (fc h) -> p fc h", fc=32),
                            in_=w2_in.rearrange("(fc p) h -> p fc h", p=128)), dC)
    CONST = P.cnt(dC)

    v(lambda e: e.tensor_copy(work[:], sc[:]), waits=[(dC, CONST)])
    wv = P.cnt(sV)
    for rnd in range(6):
        v(lambda e, rnd=rnd: e.max(out=vals[:, rnd * 8:(rnd + 1) * 8], in_=work[:]),
          waits=[(sV, wv)])
        mx_v = P.cnt(sV)
        v(lambda e, rnd=rnd: e.max_index(out=colu[:, rnd * 8:(rnd + 1) * 8],
                                         in_max=vals[:, rnd * 8:(rnd + 1) * 8],
                                         in_values=work[:]),
          waits=[(sV, mx_v)])
        mi_v = P.cnt(sV)
        v(lambda e, rnd=rnd: e.match_replace(out=work[:],
                                             in_to_replace=vals[:, rnd * 8:(rnd + 1) * 8],
                                             in_values=work[:], imm_value=-1e30),
          waits=[(sV, mi_v)])
        wv = P.cnt(sV)
    v(lambda e: e.tensor_copy(colf[:], colu[:]), waits=[(sV, wv)])
    v(lambda e: e.tensor_scalar(out=tok[:], in0=colf[:], scalar1=cst[:, 0:1],
                                scalar2=None, op0=ALU.add),
      waits=[(sV, P.cnt(sV))])
    ext_v = P.cnt(sV)

    v(lambda e: e.memset(lo_t[:], 0.0))
    v(lambda e: e.memset(hi_t[:], 1.0))
    for it in range(BIS_IT):
        v(lambda e: e.tensor_scalar(out=mid_t[:], in0=lo_t[:], scalar1=hi_t[:, 0:1],
                                    scalar2=0.5, op0=ALU.add, op1=ALU.mult),
          waits=[(sV, P.cnt(sV))])
        mid_v = P.cnt(sV)
        v(lambda e: e.tensor_scalar(out=scr[:, 0:EXT], in0=vals[:], scalar1=mid_t[:, 0:1],
                                    scalar2=None, op0=ALU.is_ge, op1=ALU.add,
                                    accum_out=cntp[:, :]),
          waits=[(sV, mid_v)])
        cnt_v = P.cnt(sV)
        t(lambda e: e.matmul(ps_m[0:1, 0:1], cntp[:, :], ones128[:, :],
                             start=True, stop=True),
          waits=[(sV, cnt_v)])
        tm1 = P.cnt(sT)
        v(lambda e: e.tensor_copy(sbrow[0:1, 0:1], ps_m[0:1, 0:1]),
          waits=[(sT, tm1)])
        sb_v = P.cnt(sV)
        t(lambda e: e.matmul(ps_m[:, 1:2], ones1_128[:, :], sbrow[0:1, 0:1],
                             start=True, stop=True),
          waits=[(sV, sb_v)])
        tm2 = P.cnt(sT)
        v(lambda e: e.tensor_copy(cbc[:], ps_m[:, 1:2]), waits=[(sT, tm2)])
        cbc_v = P.cnt(sV)
        v(lambda e: e.tensor_scalar(out=selm[:], in0=cbc[:], scalar1=float(K_SEL),
                                    scalar2=None, op0=ALU.is_ge),
          waits=[(sV, cbc_v)])
        selm_v = P.cnt(sV)
        v(lambda e: e.copy_predicated(lo_t[:], selm[:], mid_t[:]),
          waits=[(sV, selm_v)])
        v(lambda e: e.tensor_scalar(out=selm[:], in0=cbc[:], scalar1=float(K_SEL),
                                    scalar2=None, op0=ALU.is_lt),
          waits=[(sV, P.cnt(sV))])
        v(lambda e: e.copy_predicated(hi_t[:], selm[:], mid_t[:]),
          waits=[(sV, P.cnt(sV))])
    tau_v = P.cnt(sV)

    v(lambda e: e.tensor_scalar(out=mgt[:], in0=vals[:], scalar1=lo_t[:, 0:1],
                                scalar2=None, op0=ALU.is_gt, op1=ALU.add,
                                accum_out=cgt[:, :]),
      waits=[(sV, tau_v)])
    v(lambda e: e.tensor_scalar(out=meq[:], in0=vals[:], scalar1=lo_t[:, 0:1],
                                scalar2=None, op0=ALU.is_equal, op1=ALU.add,
                                accum_out=ceq[:, :]),
      waits=[(sV, P.cnt(sV))])
    mk_v = P.cnt(sV)
    t(lambda e: e.matmul(ps_m[0:1, 0:1], cgt[:, :], ones128[:, :], start=True, stop=True),
      waits=[(sV, mk_v)])
    c1t = P.cnt(sT)
    v(lambda e: e.tensor_copy(sbrow[0:1, 0:1], ps_m[0:1, 0:1]), waits=[(sT, c1t)])
    c1s_v = P.cnt(sV)
    t(lambda e: e.matmul(ps_m[:, 1:2], ones1_128[:, :], sbrow[0:1, 0:1],
                         start=True, stop=True),
      waits=[(sV, c1s_v)])
    c1t2 = P.cnt(sT)
    v(lambda e: e.tensor_copy(c1b[:], ps_m[:, 1:2]), waits=[(sT, c1t2)])
    c1_v = P.cnt(sV)
    v(lambda e: e.tensor_scalar(out=qb[:], in0=c1b[:], scalar1=-1.0,
                                scalar2=float(K_SEL), op0=ALU.mult, op1=ALU.add),
      waits=[(sV, c1_v)])
    qb_v = P.cnt(sV)
    t(lambda e: e.matmul(ps_m[:, 2:3], ut_sb[:, :], ceq[:, :], start=True, stop=True),
      waits=[(sV, mk_v)])
    be_at = P.cnt(sT)
    v(lambda e: e.tensor_copy(beq[:], ps_m[:, 2:3]), waits=[(sT, be_at)])
    be_v = P.cnt(sV)
    v(lambda e: e.memset(zer48[:], 0.0))
    z48_v = P.cnt(sV)
    v(lambda e: e.tensor_tensor_scan(out=rkeq[:], data0=meq[:], data1=zer48[:],
                                     initial=0.0, op0=ALU.add, op1=ALU.add),
      waits=[(sV, max(mk_v, z48_v))])
    v(lambda e: e.tensor_tensor(out=rkeq[:], in0=rkeq[:], in1=meq[:], op=ALU.subtract),
      waits=[(sV, P.cnt(sV))])
    v(lambda e: e.tensor_scalar(out=rkeq[:], in0=rkeq[:], scalar1=beq[:, 0:1],
                                scalar2=None, op0=ALU.add),
      waits=[(sV, P.cnt(sV)), (sV, be_v)])
    rk_v = P.cnt(sV)
    v(lambda e: e.tensor_scalar(out=scr[:], in0=rkeq[:], scalar1=qb[:, 0:1],
                                scalar2=None, op0=ALU.is_lt),
      waits=[(sV, rk_v), (sV, qb_v)])
    v(lambda e: e.tensor_tensor(out=scr[:], in0=scr[:], in1=meq[:], op=ALU.mult),
      waits=[(sV, P.cnt(sV))])
    v(lambda e: e.tensor_tensor(out=sel[:], in0=scr[:], in1=mgt[:], op=ALU.max),
      waits=[(sV, P.cnt(sV))])
    sel_v = P.cnt(sV)
    v(lambda e: e.reduce_sum(csel[:, :], sel[:], axis=AX.X), waits=[(sV, sel_v)])
    cs_v = P.cnt(sV)
    t(lambda e: e.matmul(ps_m[:, 3:4], ut_sb[:, :], csel[:, :], start=True, stop=True),
      waits=[(sV, cs_v)])
    t(lambda e: e.matmul(ps_m[:, 4:5], gut_sb[:, :], csel[:, :], start=True, stop=True))
    t(lambda e: e.matmul(ps_m[0:8, 5:6], gind_sb[:, :], csel[:, :], start=True, stop=True))
    sb_at = P.cnt(sT)
    v(lambda e: e.tensor_copy(selb[:], ps_m[:, 3:4]), waits=[(sT, sb_at)])
    v(lambda e: e.tensor_copy(grpr[:], ps_m[:, 4:5]), waits=[(sV, P.cnt(sV))])
    v(lambda e: e.tensor_copy(cnt8[:, :], ps_m[0:8, 5:6]), waits=[(sV, P.cnt(sV))])
    grp_v = P.cnt(sV)
    so(lambda e: e.dma_start(out=cnts_out[:, :], in_=cnt8[:, :]),
       oCnt, waits=[(sV, grp_v)])
    v(lambda e: e.tensor_tensor_scan(out=secp[:], data0=sel[:], data1=zer48[:],
                                     initial=0.0, op0=ALU.add, op1=ALU.add),
      waits=[(sV, grp_v)])
    v(lambda e: e.tensor_tensor(out=secp[:], in0=secp[:], in1=sel[:], op=ALU.subtract),
      waits=[(sV, P.cnt(sV))])
    v(lambda e: e.tensor_scalar(out=secp[:], in0=secp[:], scalar1=grpr[:, 0:1],
                                scalar2=None, op0=ALU.add),
      waits=[(sV, P.cnt(sV))])
    v(lambda e: e.tensor_scalar(out=secp[:], in0=secp[:], scalar1=cst[:, 1:2],
                                scalar2=None, op0=ALU.add),
      waits=[(sV, P.cnt(sV))])
    sp_v = P.cnt(sV)

    v(lambda e: e.memset(ext4[:], 0.0))
    v(lambda e: e.tensor_copy(
        ext4[:].rearrange("p (j w) -> p j w", w=4)[:, :, 0:1],
        tok[:].rearrange("p (j o) -> p j o", o=1)),
      waits=[(sV, P.cnt(sV)), (sV, ext_v)])
    v(lambda e: e.tensor_copy(
        ext4[:].rearrange("p (j w) -> p j w", w=4)[:, :, 1:2],
        vals[:].rearrange("p (j o) -> p j o", o=1)),
      waits=[(sV, P.cnt(sV))])
    v(lambda e: e.tensor_copy(
        ext4[:].rearrange("p (j w) -> p j w", w=4)[:, :, 2:3],
        secp[:].rearrange("p (j o) -> p j o", o=1)),
      waits=[(sV, P.cnt(sV)), (sV, sp_v)])
    ext4_v = P.cnt(sV)
    so(lambda e: e.dma_start(out=extbuf.ap().rearrange("(p j) w -> p (j w)", p=128),
                             in_=ext4[:]),
       oExt, waits=[(sV, ext4_v)])
    ext_st = P.cnt(oExt)

    t(lambda e: e.transpose(ps_m[0:1, 6:134], selb[:, :], idn32[:, :]),
      waits=[(sV, P.cnt(sV))])
    tr_at = P.cnt(sT)
    v(lambda e: e.tensor_copy(sbrow[:, :], ps_m[0:1, 6:134]), waits=[(sT, tr_at)])
    sbr_v = P.cnt(sV)
    t(lambda e: e.matmul(ps_m[:, 134:262], ones1_128[:, :], sbrow[:, :],
                         start=True, stop=True),
      waits=[(sV, sbr_v)])
    sbb_at = P.cnt(sT)
    v(lambda e: e.tensor_copy(sbb[:, :], ps_m[:, 134:262]), waits=[(sT, sbb_at)])
    sbb_v = P.cnt(sV)

    for k in range(16):
        v(lambda e, k=k: e.tensor_tensor(
            out=cmpm[:, :], in0=sbb[:, :],
            in1=cst[:, 2 + k:3 + k].to_broadcast([128, 128]),
            op=ALU.is_le),
          waits=[(sV, sbb_v if k == 0 else P.cnt(sV))])
        v(lambda e: e.reduce_sum(pcnt[:, :], cmpm[:, :], axis=AX.X),
          waits=[(sV, P.cnt(sV))])
        v(lambda e: e.tensor_tensor(out=cmpm[:, :], in0=sbb[:, :], in1=cmpm[:, :],
                                    op=ALU.mult),
          waits=[(sV, P.cnt(sV))])
        v(lambda e: e.reduce_max(bof[:, :], cmpm[:, :], axis=AX.X),
          waits=[(sV, P.cnt(sV))])
        v(lambda e, k=k: e.tensor_scalar(out=srcf[:, k:k + 1], in0=pcnt[:, :],
                                         scalar1=float(EXT), scalar2=-float(EXT),
                                         op0=ALU.mult, op1=ALU.add),
          waits=[(sV, P.cnt(sV))])
        v(lambda e, k=k: e.tensor_scalar(out=srcf[:, k:k + 1], in0=srcf[:, k:k + 1],
                                         scalar1=cst[:, 2 + k:3 + k], scalar2=None,
                                         op0=ALU.add),
          waits=[(sV, P.cnt(sV))])
        v(lambda e, k=k: e.tensor_scalar(out=srcf[:, k:k + 1], in0=srcf[:, k:k + 1],
                                         scalar1=bof[:, 0:1], scalar2=None,
                                         op0=ALU.subtract),
          waits=[(sV, P.cnt(sV))])
    v(lambda e: e.tensor_copy(srcu[:, :], srcf[:, :]), waits=[(sV, P.cnt(sV))])
    src_v = P.cnt(sV)

    for k in range(16):
        g(lambda e, k=k: e.indirect_dma_start(
            out=cmp4[:, 4 * k:4 * k + 4],
            out_offset=None,
            in_=extbuf.ap(),
            in_offset=IndirectOffsetOnAxis(srcu[:, k:k + 1], axis=0)),
          dGc, waits=[(sV, src_v), (oExt, ext_st)] if k == 0 else [])
    cmp_g = P.cnt(dGc)
    v(lambda e: e.tensor_copy(
        toku[:, :], cmp4[:].rearrange("p (k w) -> p k w", w=4)[:, :, 0:1]),
      waits=[(dGc, cmp_g)])
    v(lambda e: e.tensor_copy(
        secu[:, :], cmp4[:].rearrange("p (k w) -> p k w", w=4)[:, :, 2:3]),
      waits=[(sV, P.cnt(sV))])
    idx_v = P.cnt(sV)

    for k in range(16):
        r = k % 2
        tposed_at = P.cnt(sT)
        g(lambda e, k=k, r=r: e.indirect_dma_start(
            out=toks[r][:, :],
            out_offset=None,
            in_=x1b_in,
            in_offset=IndirectOffsetOnAxis(toku[:, k:k + 1], axis=0)),
          dGx[r], waits=[(sV, idx_v), (sT, tposed_at)])
        gk = P.cnt(dGx[r])
        evac_prior = P.cnt(sV)
        for cc in range(8):
            t(lambda e, r=r, cc=cc: e.transpose(
                ps_t[:, cc * 128:(cc + 1) * 128], toks[r][:, cc * 128:(cc + 1) * 128],
                idn[:, :]),
              waits=([(dGx[r], gk), (sV, evac_prior), (dC, CONST)] if cc == 0 else []),
              inc=(cc == 7))
        tr_at = P.cnt(sT)
        v(lambda e, k=k: e.tensor_copy(
            toksT[:].rearrange("p (cc n) -> p cc n", cc=8)[:, :, k * 128:(k + 1) * 128],
            ps_t[:].rearrange("p (cc k) -> p cc k", cc=8)),
          waits=[(sT, tr_at)])
    toksT_v = P.cnt(sV)

    for half in range(2):
        for fb in range(32):
            r = fb % 2
            g(lambda e, fb=fb, r=r: e.dma_start(
                out=w1s[r][:].rearrange("p (cc d) -> p cc d", cc=8),
                in_=w1_in[:, fb * 128:(fb + 1) * 128].rearrange(
                    "(cc p) d -> p cc d", p=128)),
              dW1[r], waits=[(sT, P.cnt(sT))] if (half, fb) > (0, 1) else [])
            w1_at = P.cnt(dW1[r])
            relu_prior = P.cnt(sA)
            for cc in range(8):
                for tc in range(2):
                    t(lambda e, r=r, cc=cc, tc=tc, half=half: e.matmul(
                        ps_h1[:, tc * 512:(tc + 1) * 512],
                        w1s[r][:, cc * 128:(cc + 1) * 128],
                        toksT[:, cc * 2048 + half * 1024 + tc * 512:
                              cc * 2048 + half * 1024 + tc * 512 + 512],
                        start=(cc == 0), stop=(cc == 7)),
                      waits=([(sV, toksT_v), (dW1[r], w1_at), (sA, relu_prior)]
                             if (cc == 0 and tc == 0) else []),
                      inc=(cc == 7 and tc == 1))
            h1_at = P.cnt(sT)
            a(lambda e, fb=fb: e.activation(
                out=h1T[:, fb * 1024:(fb + 1) * 1024], in_=ps_h1[:, :],
                func=AF.Relu, bias=b1_sb[:, fb:fb + 1]),
              waits=[(sT, h1_at), (sA, P.cnt(sA))])
        relu_done = P.cnt(sA)
        for k8 in range(8):
            k = half * 8 + k8
            r = k8 % 2
            ev_prior = P.cnt(sV)
            for fc in range(32):
                t(lambda e, fc=fc, k8=k8: e.matmul(
                    ps_eo[:, 0:512],
                    h1T[:, fc * 1024 + k8 * 128: fc * 1024 + k8 * 128 + 128],
                    w2_sb[:, fc * 1024: fc * 1024 + 512],
                    start=(fc == 0), stop=False),
                  waits=([(sA, relu_done), (sV, ev_prior)] if fc == 0 else []),
                  inc=False)
                t(lambda e, fc=fc, k8=k8: e.matmul(
                    ps_eo[:, 512:1024],
                    h1T[:, fc * 1024 + k8 * 128: fc * 1024 + k8 * 128 + 128],
                    w2_sb[:, fc * 1024 + 512: fc * 1024 + 1024],
                    start=(fc == 0), stop=False),
                  inc=False)
            t(lambda e: e.matmul(ps_eo[:, 0:512], onesc[:, :], b2_sb[:, 0:512],
                                 start=False, stop=True),
              inc=False)
            t(lambda e: e.matmul(ps_eo[:, 512:1024], onesc[:, :], b2_sb[:, 512:1024],
                                 start=False, stop=True))
            eo_at = P.cnt(sT)
            v(lambda e, r=r, k=k: e.tensor_scalar(
                out=contrib[r][:, 0:1024], in0=ps_eo[:, :],
                scalar1=cmp4[:, 4 * k + 1:4 * k + 2], scalar2=None, op0=ALU.mult),
              waits=[(sT, eo_at), (dSc[r], P.cnt(dSc[r]))])
            v(lambda e, r=r, k=k: e.tensor_copy(
                contrib[r][:, 1024:1025], cmp4[:, 4 * k:4 * k + 1]),
              waits=[(sV, P.cnt(sV))])
            v(lambda e, r=r: e.memset(contrib[r][:, 1025:1028], 0.0),
              waits=[(sV, P.cnt(sV))])
            c_v = P.cnt(sV)
            g(lambda e, r=r, k=k: e.indirect_dma_start(
                out=secs_out,
                out_offset=IndirectOffsetOnAxis(secu[:, k:k + 1], axis=0),
                in_=contrib[r][:, :],
                in_offset=None,
                bounds_check=E * SEC - 1,
                oob_is_err=True),
              dSc[r], waits=[(sV, c_v)])

    P.emit("sync", lambda e: None,
           waits=[(oCnt, P.cnt(oCnt)), (dSc[0], P.cnt(dSc[0])),
                  (dSc[1], P.cnt(dSc[1]))], inc=None)
    attach(P, block)
    st.close()
    return nc
def build_l3():
    nc = bass.Bass("TRN2", target_bir_lowering=False, debug=False)
    x1f_in = nc.dram_tensor("x1f", [B_PER * S, C], F32, kind="ExternalInput").ap()
    secs_in = nc.dram_tensor("secs", [E * SEC, 1028], F32, kind="ExternalInput").ap()
    cnts_in = nc.dram_tensor("cnts", [1, E], F32, kind="ExternalInput").ap()
    cst_in = nc.dram_tensor("cst", [128, 8], F32, kind="ExternalInput").ap()
    onesr_in = nc.dram_tensor("onesr", [1, 128], F32, kind="ExternalInput").ap()
    yout = nc.dram_tensor("y", [B_PER * S, C], F32, kind="ExternalOutput").ap()
    ybuf = nc.dram_tensor("ybuf", [B_PER * S + SEC, C], F32)

    P = Plan()
    st = ExitStack()
    En = st.enter_context
    secs_sb = [En(nc.sbuf_tensor(f"secs_sb{i}", [128, NOPS * 1028], F32)) for i in range(2)]
    cnt_sb = En(nc.sbuf_tensor("cnt_sb", [1, E], F32))
    cnt_bc = En(nc.sbuf_tensor("cnt_bc", [128, E], F32))
    cst = En(nc.sbuf_tensor("cst_sb", [128, 8], F32))
    ones1_128 = En(nc.sbuf_tensor("ones1_128", [1, 128], F32))
    idxf = [En(nc.sbuf_tensor(f"idxf{i}", [128, NOPS], F32)) for i in range(2)]
    valid = En(nc.sbuf_tensor("valid", [128, NOPS], U32))
    idxu = [En(nc.sbuf_tensor(f"idxu{i}", [128, NOPS], U32)) for i in range(2)]
    z_t = [En(nc.sbuf_tensor(f"z_t{i}", [128, C], F32)) for i in range(2)]
    o_t = [En(nc.sbuf_tensor(f"o_t{i}", [128, C], F32)) for i in range(2)]
    sqj = En(nc.sbuf_tensor("sqj", [128, C], F32))
    mean_t = En(nc.sbuf_tensor("mean_t", [128, 1], F32))
    ss_t = En(nc.sbuf_tensor("ss_t", [128, 1], F32))
    lv_t = En(nc.sbuf_tensor("lv_t", [128, 1], F32))
    rstd_t = En(nc.sbuf_tensor("rstd_t", [128, 1], F32))
    eps_t = En(nc.sbuf_tensor("eps_t", [128, 1], F32))
    ps_m = En(nc.psum_tensor("ps_m", [128, 512], F32))

    dC = En(nc.semaphore("dC"))
    dPre = En(nc.semaphore("dPre"))
    dSec = [En(nc.semaphore(f"dSec{i}")) for i in range(2)]
    dScat = En(nc.semaphore("dScat"))
    dZ = [En(nc.semaphore(f"dZ{i}")) for i in range(2)]
    oY = [En(nc.semaphore(f"oY{i}")) for i in range(2)]
    sT = En(nc.semaphore("sT"))
    sV = En(nc.semaphore("sV"))
    sA = En(nc.semaphore("sA"))
    block = En(nc.Block())

    g, so, t, v, a = mkhelpers(P, sT, sV, sA)

    v(lambda e: e.memset(eps_t[:], LN_EPS))
    g(lambda e: e.dma_start(out=cnt_sb[:], in_=cnts_in[:, :]), dC)
    g(lambda e: e.dma_start(out=cst[:], in_=cst_in[:, :]), dC)
    g(lambda e: e.dma_start(out=ones1_128[:], in_=onesr_in[:, :]), dC)
    CONST = P.cnt(dC)
    for ch in range(8):
        g(lambda e, ch=ch: e.dma_start(
            out=ybuf[ch * 1024:(ch + 1) * 1024, :],
            in_=x1f_in[ch * 1024:(ch + 1) * 1024, :]), dPre)
    pre_g = P.cnt(dPre)

    t(lambda e: e.matmul(ps_m[:, 0:8], ones1_128[:, :], cnt_sb[:, :],
                         start=True, stop=True),
      waits=[(dC, CONST)])
    cb_at = P.cnt(sT)
    v(lambda e: e.tensor_copy(cnt_bc[:, :], ps_m[:, 0:8]), waits=[(sT, cb_at)])
    cb_v = P.cnt(sV)

    for e_ix in range(E):
        r = e_ix % 2
        g(lambda e, e_ix=e_ix, r=r: e.dma_start(
            out=secs_sb[r][:].rearrange("p (op w) -> p op w", op=NOPS),
            in_=secs_in[e_ix * SEC:(e_ix + 1) * SEC, :].rearrange(
                "(op p) w -> p op w", p=128)),
          dSec[r], waits=[(dScat, P.cnt(dScat))])
        sec_at = P.cnt(dSec[r])
        v(lambda e, r=r: e.tensor_scalar(
            out=idxf[r][:, :],
            in0=secs_sb[r][:].rearrange("p (op w) -> p op w", op=NOPS)[:, :, 1024:1025],
            scalar1=cst[:, 0:1], scalar2=None, op0=ALU.subtract),
          waits=[(dSec[r], sec_at), (sV, P.cnt(sV))])
        v(lambda e, e_ix=e_ix: e.tensor_scalar(
            out=valid[:, :], in0=cst[:, 4:7],
            scalar1=cnt_bc[:, e_ix:e_ix + 1], scalar2=None, op0=ALU.is_ge),
          waits=[(sV, max(P.cnt(sV), cb_v))])
        v(lambda e, r=r: e.copy_predicated(idxf[r][:, :], valid[:, :], cst[:, 1:4]),
          waits=[(sV, P.cnt(sV))])
        v(lambda e, r=r: e.tensor_copy(idxu[r][:, :], idxf[r][:, :]),
          waits=[(sV, P.cnt(sV))])
        i_v = P.cnt(sV)
        for op in range(NOPS):
            g(lambda e, r=r, op=op: e.indirect_dma_start(
                out=ybuf.ap(),
                out_offset=IndirectOffsetOnAxis(idxu[r][:, op:op + 1], axis=0),
                in_=secs_sb[r][:, op * 1028: op * 1028 + 1024],
                in_offset=None,
                bounds_check=B_PER * S + SEC - 1,
                oob_is_err=True,
                compute_op=ALU.add),
              dScat,
              waits=([(sV, i_v), (dPre, pre_g)] if op == 0 else []))
    scat_g = P.cnt(dScat)

    for tb in range(B_PER * NB):
        r = tb % 2
        g(lambda e, tb=tb, r=r: e.dma_start(
            out=z_t[r][:], in_=ybuf[tb * 128:(tb + 1) * 128, :]),
          dZ[r], waits=[(sV, P.cnt(sV)), (dScat, scat_g)])
        ld_at = P.cnt(dZ[r])
        v(lambda e, r=r: e.reduce_sum(mean_t[:, :], z_t[r][:], axis=AX.X),
          waits=[(dZ[r], ld_at), (sV, P.cnt(sV))])
        v(lambda e: e.tensor_scalar(out=mean_t[:, :], in0=mean_t[:, :],
                                    scalar1=1.0 / C, scalar2=None, op0=ALU.mult),
          waits=[(sV, P.cnt(sV))])
        v(lambda e, r=r: e.tensor_scalar(out=z_t[r][:], in0=z_t[r][:],
                                         scalar1=mean_t[:, 0:1], scalar2=None,
                                         op0=ALU.subtract),
          waits=[(sV, P.cnt(sV))])
        zc_v = P.cnt(sV)
        a(lambda e, r=r: e.activation(out=sqj[:, :], in_=z_t[r][:], func=AF.Square,
                                      accum_out=ss_t[:, :]),
          waits=[(sV, zc_v), (sA, P.cnt(sA))])
        a(lambda e: e.activation(out=lv_t[:, :], in_=ss_t[:, :], func=AF.Ln,
                                 scale=1.0 / C, bias=eps_t[:, 0:1]),
          waits=[(sA, P.cnt(sA))])
        a(lambda e: e.activation(out=rstd_t[:, :], in_=lv_t[:, :], func=AF.Exp,
                                 scale=-0.5),
          waits=[(sA, P.cnt(sA))])
        rstd_a = P.cnt(sA)
        v(lambda e, r=r: e.tensor_scalar(out=o_t[r][:], in0=z_t[r][:],
                                         scalar1=rstd_t[:, 0:1], scalar2=None,
                                         op0=ALU.mult),
          waits=[(sA, rstd_a), (sV, P.cnt(sV)), (oY[r], P.cnt(oY[r]))])
        o_v = P.cnt(sV)
        so(lambda e, tb=tb, r=r: e.dma_start(
            out=yout[tb * 128:(tb + 1) * 128, :], in_=o_t[r][:]),
           oY[r], waits=[(sV, o_v)])

    P.emit("sync", lambda e: None,
           waits=[(oY[0], P.cnt(oY[0])), (oY[1], P.cnt(oY[1]))], inc=None)
    attach(P, block)
    st.close()
    return nc
_cache = {}


def _get(name, builder):
    if name not in _cache:
        _cache[name] = builder()
    return _cache[name]


def kernel(**inputs):
    f32 = np.float32
    x = np.ascontiguousarray(np.asarray(inputs["L0_attn_x"], f32))
    Wq = np.ascontiguousarray(np.asarray(inputs["L0_attn_Wq"], f32))
    Wk = np.ascontiguousarray(np.asarray(inputs["L0_attn_Wk"], f32))
    bq = np.asarray(inputs["L0_attn_bq"], f32)
    bk = np.asarray(inputs["L0_attn_bk"], f32)
    rw = np.ascontiguousarray(np.asarray(inputs["L0_ffn_router_w"], f32))
    rb = np.asarray(inputs["L0_ffn_router_b"], f32)
    w1 = np.ascontiguousarray(np.asarray(inputs["L0_ffn_w1"], f32))
    b1 = np.asarray(inputs["L0_ffn_b1"], f32)
    w2 = np.ascontiguousarray(np.asarray(inputs["L0_ffn_w2"], f32))
    b2 = np.asarray(inputs["L0_ffn_b2"], f32)

    idn = np.eye(128, dtype=f32)
    ones128x8 = np.ones((128, 8), dtype=f32)
    bq_r = bq.reshape(8, 128).T.copy()
    bk_r = bk.reshape(8, 128).T.copy()
    rb_r = rb.reshape(E, 1).copy()

    nc1 = _get("l1", build_l1)
    in1 = []
    for c in range(NCORES):
        in1.append({
            "xin": x[B_PER * c:B_PER * (c + 1)],
            "wq": Wq, "wk": Wk, "bq_r": bq_r, "bk_r": bk_r,
            "rw": rw, "rb_r": rb_r, "idn_in": idn, "ones_in": ones128x8,
        })
    r1 = run_bass_kernel_spmd(nc1, in1, core_ids=list(range(NCORES)))
    x1f = [r1.results[c]["x1f"] for c in range(NCORES)]
    x1b = [r1.results[c]["x1b"] for c in range(NCORES)]
    scs = [r1.results[c]["sc"] for c in range(NCORES)]

    sc_all = np.concatenate(scs, axis=1)
    x1b_full = np.concatenate(x1b, axis=0)
    p_idx = np.arange(128, dtype=f32)
    cst2 = np.zeros((128, 24), dtype=f32)
    cst2[:, 0] = p_idx * 512
    cst2[:, 1] = (p_idx // 16).astype(f32) * SEC
    for k in range(16):
        cst2[:, 2 + k] = k * 128 + p_idx
    cst2[:, 18] = 1.0
    ut = np.triu(np.ones((128, 128), dtype=f32), 1)
    gut = np.zeros((128, 128), dtype=f32)
    for gix in range(8):
        blkk = np.triu(np.ones((16, 16), dtype=f32), 1)
        gut[gix * 16:(gix + 1) * 16, gix * 16:(gix + 1) * 16] = blkk
    gind = np.zeros((128, 8), dtype=f32)
    gind[np.arange(128), (np.arange(128) // 16)] = 1.0

    nc2 = _get("l2", build_l2)
    in2 = []
    for e_ix in range(NCORES):
        in2.append({
            "sc": np.ascontiguousarray(sc_all[e_ix].reshape(128, 512)),
            "x1b": x1b_full,
            "w1": np.ascontiguousarray(w1[e_ix]),
            "b1_r": np.ascontiguousarray(b1[e_ix].reshape(32, 128).T),
            "w2": np.ascontiguousarray(w2[e_ix]),
            "b2_r": np.ascontiguousarray(b2[e_ix].reshape(1, C)),
            "idn_in": idn, "cst": cst2, "ut_in": ut, "gut_in": gut,
            "gind_in": gind, "onesr": np.ones((1, 128), dtype=f32),
        })
    r2 = run_bass_kernel_spmd(nc2, in2, core_ids=list(range(NCORES)))
    secs = [r2.results[e_ix]["secs"] for e_ix in range(NCORES)]
    cnts = [r2.results[e_ix]["cnts"] for e_ix in range(NCORES)]

    nc3 = _get("l3", build_l3)
    in3 = []
    for c in range(NCORES):
        sec_c = np.concatenate([secs[e_ix][c * SEC:(c + 1) * SEC] for e_ix in range(E)],
                               axis=0)
        cnt_c = np.array([[cnts[e_ix][c, 0] for e_ix in range(E)]], dtype=f32)
        cst3 = np.zeros((128, 8), dtype=f32)
        cst3[:, 0] = 8192.0 * c
        for op in range(NOPS):
            cst3[:, 1 + op] = B_PER * S + op * 128 + p_idx
            cst3[:, 4 + op] = p_idx + 128 * op
        cst3[:, 7] = 1.0
        in3.append({
            "x1f": x1f[c], "secs": sec_c, "cnts": cnt_c, "cst": cst3,
            "onesr": np.ones((1, 128), dtype=f32),
        })
    r3 = run_bass_kernel_spmd(nc3, in3, core_ids=list(range(NCORES)))
    out = np.concatenate([r3.results[c]["y"] for c in range(NCORES)], axis=0)
    return out.reshape(NCORES * B_PER, S, C).astype(np.float32)
